# version 02 (final)
# baseline (speedup 1.0000x reference)
import numpy as np
import ml_dtypes
import concourse.bass as bass
import concourse.mybir as mybir
from concourse.bass_utils import run_bass_kernel_spmd

F32 = mybir.dt.float32
BF16 = mybir.dt.bfloat16
AF = mybir.ActivationFunctionType
ALU = mybir.AluOpType
AX = mybir.AxisListType
BF = ml_dtypes.bfloat16

T = 2048
NT = 16
DC = 8
KB = 1024
SB0 = 16512


def sb_at(nc, name, shape, dt, offset):
    return nc.alloc_sbuf_tensor_at(name, shape, dt, offset=offset + SB0)

def host_consts():
    c = {}
    t = np.arange(T, dtype=np.float64)
    D = 1024
    half = D // 2
    scale_vec = (np.arange(0, D, 2) + 0.4 * D) / (1.4 * D)
    power = t / 512.0
    scale = scale_vec[None, :] ** power[:, None]
    inv_freq = 1.0 / 10000.0 ** (np.arange(half) / half)
    sinus = t[:, None] * inv_freq[None, :]
    sn, cs = np.sin(sinus), np.cos(sinus)
    rep = lambda a: np.repeat(a, 2, axis=1)
    c["l0cq"] = rep(cs * scale).T
    c["l0sq"] = rep(sn * scale).T
    c["l0ck"] = rep(cs / scale).T
    c["l0sk"] = rep(sn / scale).T
    p = np.arange(128)[:, None]
    j = np.arange(3968)[None, :]
    c["dect"] = 0.9 ** np.abs(p + 1920.0 - j)
    fr = np.exp(-np.log(10000.0) * np.arange(0, 64, 2) / 64.0)
    ang = np.repeat(t[:, None] * fr[None, :], 2, axis=1)
    cs1, sn1 = np.cos(ang).T, np.sin(ang).T
    two = lambda a: np.concatenate([a, a], axis=0)
    c["acq"] = two(cs1) * 0.125
    c["asq"] = two(sn1) * 0.125
    c["ack"] = two(cs1)
    c["ask"] = two(sn1)
    hidx = np.arange(0, 64, 2, dtype=np.float64)
    invf = 1.0 / 10000.0 ** (hidx / 64.0)
    fr2 = t[:, None] * invf[None, :]
    fr2 = np.concatenate([fr2, fr2], axis=1)
    base = (hidx + 0.4 * 64) / (1.4 * 64)
    pw = (t - 1024.0) / 256.0
    sc2 = base[None, :] ** pw[:, None]
    sc2 = np.concatenate([sc2, sc2], axis=1)
    c["fcq"] = two((np.cos(fr2) * sc2).T) * 8.0
    c["fsq"] = two((np.sin(fr2) * sc2).T) * 8.0
    c["fck"] = two((np.cos(fr2) / sc2).T)
    c["fsk"] = two((np.sin(fr2) / sc2).T)
    P1 = np.zeros((128, 128))
    for i in range(64):
        P1[2 * i, 2 * i + 1] = -1.0
        P1[2 * i + 1, 2 * i] = 1.0
    c["pint"] = P1.T.copy()
    P2 = np.zeros((128, 128))
    for b0 in (0, 64):
        for i in range(32):
            P2[b0 + i, b0 + i + 32] = -1.0
            P2[b0 + i + 32, b0 + i] = 1.0
    c["psplit"] = P2.T.copy()
    c["ident"] = np.eye(128)
    bf16 = {k: np.ascontiguousarray(v.astype(BF)) for k, v in c.items()}
    f32 = {}
    f32["identf"] = np.eye(128, dtype=np.float32)
    r = np.arange(128)
    f32["diagm"] = np.where(r[None, :] <= r[:, None], 0.0, -1e30).astype(np.float32)
    jj = np.arange(1408)[None, :]
    pp = np.arange(128)[:, None]
    dd = (jj - 384) - pp
    f32["fmask"] = np.where((dd >= 0) & (dd <= 512), 0.0, -1e30).astype(np.float32)
    return bf16, f32


def host_layout(inputs):
    g = {}
    g["wq"] = inputs["L0_attn_W_Q"]
    g["wk"] = inputs["L0_attn_W_K"]
    g["wv"] = inputs["L0_attn_W_V"]
    g["wrt"] = np.ascontiguousarray(inputs["L0_ffn_Wr"].T)
    g["br"] = np.ascontiguousarray(inputs["L0_ffn_br"].reshape(1, 8))
    g["w1t"] = np.ascontiguousarray(inputs["L0_ffn_W1"].transpose(0, 2, 1))
    g["b1c"] = np.ascontiguousarray(
        inputs["L0_ffn_b1"].reshape(8, 32, 128).transpose(2, 0, 1).reshape(128, 256))
    g["w2t"] = np.ascontiguousarray(inputs["L0_ffn_W2"].transpose(0, 2, 1))
    g["b2r"] = np.ascontiguousarray(inputs["L0_ffn_b2"].reshape(8, 1024))
    g["wqkvt"] = np.ascontiguousarray(inputs["L1_attn_Wqkv"].T)
    g["wot"] = np.ascontiguousarray(inputs["L1_attn_Wo"].T)
    g["fwqkv"] = inputs["L1_ffn_wqkv"]
    qs = inputs["L1_ffn_q_scale"]
    ks = inputs["L1_ffn_k_scale"]
    g["qsks"] = np.ascontiguousarray(
        np.stack([np.tile(qs[0], 2), np.tile(ks[0], 2),
                  np.tile(qs[1], 2), np.tile(ks[1], 2)], axis=1))
    g["fwo"] = inputs["L1_ffn_wo"]
    g["lng"] = np.ascontiguousarray(
        inputs["L1_ffn_ln_g"].reshape(2, 8, 128).transpose(2, 0, 1).reshape(128, 16))
    g["lnb"] = np.ascontiguousarray(
        inputs["L1_ffn_ln_b"].reshape(2, 8, 128).transpose(2, 0, 1).reshape(128, 16))
    w1 = inputs["L1_ffn_w1"]
    HID = 2730
    FP = 2816
    w1a = np.zeros((2, 1024, FP), np.float32); w1a[:, :, :HID] = w1[:, :, :HID]
    w1g = np.zeros((2, 1024, FP), np.float32); w1g[:, :, :HID] = w1[:, :, HID:]
    g["fw1a"] = w1a
    g["fw1g"] = w1g
    b1 = inputs["L1_ffn_b1"]
    b1a = np.zeros((2, FP), np.float32); b1a[:, :HID] = b1[:, :HID]
    b1g = np.zeros((2, FP), np.float32); b1g[:, :HID] = b1[:, HID:]
    g["fb1a"] = np.ascontiguousarray(b1a.reshape(2, 22, 128).transpose(2, 0, 1).reshape(128, 44))
    g["fb1g"] = np.ascontiguousarray(b1g.reshape(2, 22, 128).transpose(2, 0, 1).reshape(128, 44))
    w2 = inputs["L1_ffn_w2"]
    w2p = np.zeros((2, FP, 1024), np.float32); w2p[:, :HID, :] = w2
    g["fw2"] = w2p
    g["fb2"] = np.ascontiguousarray(inputs["L1_ffn_b2"].reshape(1, 2048))
    return g


class PB:
    KEYS = ("T", "V", "S", "G", "QS", "QG", "QC")
    ENG = {"T": "tensor", "V": "vector", "S": "scalar", "G": "gpsimd",
           "QS": "sync", "QG": "gpsimd", "QC": "scalar"}

    def __init__(self):
        self.ops = []
        self.cnt = dict.fromkeys(self.KEYS, 0)
        self.pend = {}
        self.resw = {}
        self.resr = {}

    def emit(self, key, method, *args, waits=(), rd=(), wr=(), **kw):
        wl = [w for w in waits if w is not None]
        for n in rd:
            if n in self.resw:
                wl.append(self.resw[n])
        for n in wr:
            wl += self.resr.get(n, [])
            if n in self.resw:
                wl.append(self.resw[n])
        step = 16 if key.startswith("Q") else 1
        self.cnt[key] += step
        stream = self.ENG[key]
        if stream in self.pend:
            wl = self.pend.pop(stream) + wl
        tok = (key, self.cnt[key])
        self.ops.append((key, method, args, kw, wl))
        for n in rd:
            self.resr.setdefault(n, []).append(tok)
        for n in wr:
            self.resw[n] = tok
            self.resr[n] = []
        return tok

    def barrier(self):
        snap = [(k, self.cnt[k]) for k in self.KEYS if self.cnt[k] > 0]
        for s in set(self.ENG.values()):
            self.pend[s] = list(snap)

    def T(self, m, *a, **k): return self.emit("T", m, *a, **k)
    def V(self, m, *a, **k): return self.emit("V", m, *a, **k)
    def S(self, m, *a, **k): return self.emit("S", m, *a, **k)
    def G(self, m, *a, **k): return self.emit("G", m, *a, **k)


def replay(nc, pb):
    import contextlib
    stack = contextlib.ExitStack()
    sems = {k: stack.enter_context(nc.semaphore("sem" + k)) for k in PB.KEYS}
    streams = {s: [] for s in set(PB.ENG.values())}
    for op in pb.ops:
        streams[PB.ENG[op[0]]].append(op)
    waited_vals = {k: set() for k in PB.KEYS if k.startswith("Q")}
    for (_k, _m, _a, _kw, waits) in pb.ops:
        for (wk, wv) in waits:
            if wk.startswith("Q"):
                waited_vals[wk].add(wv)

    def runner(ops):
        def f(e):
            waited = {}
            qcnt = {}
            ccnt = {}
            pend_self = {}
            for (key, method, args, kw, waits) in ops:
                mystream = PB.ENG[key]
                if key.startswith("Q"):
                    v = pend_self.pop(key, None)
                    if v is not None and waited.get(key, 0) < v:
                        e.wait_ge(sems[key], v)
                        waited[key] = v
                elif key in ("V", "S", "G"):
                    prev = ccnt.get(key, 0)
                    if prev > 0 and waited.get(key, 0) < prev:
                        e.wait_ge(sems[key], prev)
                        waited[key] = prev
                for (wk, wv) in waits:
                    if (PB.ENG[wk] == mystream) and not wk.startswith("Q"):
                        continue
                    if waited.get(wk, 0) >= wv:
                        continue
                    e.wait_ge(sems[wk], wv)
                    waited[wk] = wv
                m = method
                if m == "tensor_copy" and key == "S":
                    m = "copy"
                ins = getattr(e, m)(*args, **kw)
                ins.then_inc(sems[key], 16 if key.startswith("Q") else 1)
                if key.startswith("Q"):
                    qcnt[key] = qcnt.get(key, 0) + 16
                    if qcnt[key] in waited_vals[key]:
                        pend_self[key] = qcnt[key]
                else:
                    ccnt[key] = ccnt.get(key, 0) + 1
        return f

    with nc.Block() as blk:
        blk.tensor(runner(streams["tensor"]))
        blk.vector(runner(streams["vector"]))
        blk.scalar(runner(streams["scalar"]))
        blk.gpsimd(runner(streams["gpsimd"]))
        blk.sync(runner(streams["sync"]))
    stack.close()


class Zone:
    def __init__(self, nc, base, end):
        self.nc, self.cur, self.end = nc, base, end

    def al(self, name, shape, dt):
        sz = int(np.prod(shape[1:])) * (4 if dt == F32 else 2)
        sz = (sz + 31) & ~31
        t = sb_at(self.nc, name, shape, dt, self.cur)
        self.cur += sz
        assert self.cur <= self.end, f"zone overflow at {name}: {self.cur} > {self.end}"
        return t


def build_program(upto=99, taps=()):
    nc = bass.Bass("TRN2", target_bir_lowering=False, detect_race_conditions=False)
    pb = PB()
    cbf, cf32 = host_consts()

    dIn = lambda n, shp, dt=F32: nc.dram_tensor(n, shp, dt, kind="ExternalInput").ap()
    x_in = dIn("x0", [T, 1024])
    wq_in = dIn("wq", [1024, 1024]); wk_in = dIn("wk", [1024, 1024]); wv_in = dIn("wv", [1024, 1024])
    wrt_in = dIn("wrt", [1024, 8]); br_in = dIn("br", [1, 8])
    w1t_in = dIn("w1t", [8, 1024, 4096]); b1c_in = dIn("b1c", [128, 256])
    w2t_in = dIn("w2t", [8, 4096, 1024]); b2r_in = dIn("b2r", [8, 1024])
    wqkvt_in = dIn("wqkvt", [1024, 3072]); wot_in = dIn("wot", [1024, 1024])
    fwqkv_in = dIn("fwqkv", [2, 1024, 1536]); fwo_in = dIn("fwo", [2, 512, 1024])
    fw1a_in = dIn("fw1a", [2, 1024, 2816]); fw1g_in = dIn("fw1g", [2, 1024, 2816])
    fw2_in = dIn("fw2", [2, 2816, 1024])
    fb1a_in = dIn("fb1a", [128, 44]); fb1g_in = dIn("fb1g", [128, 44])
    fb2_in = dIn("fb2", [1, 2048])
    lng_in = dIn("lng", [128, 16]); lnb_in = dIn("lnb", [128, 16])
    qsks_in = dIn("qsks", [128, 4])
    cin = {}
    for n, a in list(cbf.items()):
        cin[n] = dIn(n, list(a.shape), BF16)
    for n, a in list(cf32.items()):
        cin[n] = dIn(n, list(a.shape), F32)
    y_out = nc.dram_tensor("y", [T, 1024], F32, kind="ExternalOutput").ap()
    xres = nc.dram_tensor("xres", [T, 1024], F32, kind="Internal").ap()
    tap_out = {}
    def tap(name, shape, dt=F32):
        tap_out[name] = nc.dram_tensor("tap_" + name, shape, dt, kind="ExternalOutput").ap()
        return tap_out[name]

    ZP = Zone(nc, 176 * KB, 192 * KB)
    ident = ZP.al("ident", [128, 128], BF16)
    identf = ZP.al("identf", [128, 128], F32)
    pint = ZP.al("pint", [128, 128], BF16)
    psplit = ZP.al("psplit", [128, 128], BF16)
    diagm = ZP.al("diagm", [128, 128], F32)
    fmask = ZP.al("fmask", [128, 1408], F32)
    ones_row = ZP.al("ones_row", [1, 512], BF16)
    lng_sb = ZP.al("lng_sb", [128, 16], F32)
    lnb_sb = ZP.al("lnb_sb", [128, 16], F32)
    qsks_sb = ZP.al("qsks_sb", [128, 4], F32)
    fb1a_sb = ZP.al("fb1a_sb", [128, 44], F32)
    fb1g_sb = ZP.al("fb1g_sb", [128, 44], F32)
    fb2_sb = ZP.al("fb2_sb", [1, 2048], BF16)
    b1c_sb = ZP.al("b1c_sb", [128, 256], F32)
    wrth = ZP.al("wrth", [128, 64], BF16)
    wrtl = ZP.al("wrtl", [128, 64], BF16)
    brh = ZP.al("brh", [1, 8], BF16)
    brl = ZP.al("brl", [1, 8], BF16)
    wrf = ZP.al("wrf", [128, 64], F32)
    brf = ZP.al("brf", [1, 8], F32)
    mcol = [ZP.al(f"mcol{i}", [128, 1], F32) for i in range(2)]
    scol = [ZP.al(f"scol{i}", [128, 1], F32) for i in range(2)]
    sscol = [ZP.al(f"sscol{i}", [128, 1], F32) for i in range(2)]
    v1col = [ZP.al(f"v1col{i}", [128, 1], F32) for i in range(2)]
    s1c = [ZP.al(f"s1c{i}", [128, 1], F32) for i in range(2)]
    s2c = [ZP.al(f"s2c{i}", [128, 1], F32) for i in range(2)]
    nmcol = ZP.al("nmcol", [128, 1], F32)
    dncol = ZP.al("dncol", [128, 1], F32)
    rdcol = ZP.al("rdcol", [128, 1], F32)
    qss = ZP.al("qss", [128, 8], F32)
    qrn = ZP.al("qrn", [128, 8], F32)
    lgmax = ZP.al("lgmax", [128, 1], F32)
    ohtmp = ZP.al("ohtmp", [128, 8], F32)
    ohc = ZP.al("ohc", [128, 8], F32)
    oh = ZP.al("oh", [128, 128], F32)

    psA = nc.alloc_psum_tensor("psA", [128, 2048], F32)
    psB = nc.alloc_psum_tensor("psB", [128, 1024], F32)
    psT1 = nc.alloc_psum_tensor("psT1", [128, 1024], BF16)
    psT2 = nc.alloc_psum_tensor("psT2", [128, 1024], BF16)
    psTs = [psT1, psT2]

    xnT = sb_at(nc, "xnT", [128, DC, T], BF16, 0)

    for dst, src in ((ident, cin["ident"]), (identf, cin["identf"]), (pint, cin["pint"]),
                     (psplit, cin["psplit"]), (diagm, cin["diagm"]), (fmask, cin["fmask"]),
                     (lng_sb, lng_in), (lnb_sb, lnb_in), (qsks_sb, qsks_in),
                     (fb1a_sb, fb1a_in), (fb1g_sb, fb1g_in),
                     (b1c_sb, b1c_in), (brf, br_in)):
        pb.emit("QC", "dma_start", out=dst[:], in_=src[:])
    pb.emit("QG", "dma_start", out=fb2_sb[:], in_=fb2_in[:])
    for c in range(8):
        tk_c = pb.emit("QC", "dma_start", out=wrf[:, c * 8:(c + 1) * 8],
                       in_=wrt_in[c * 128:(c + 1) * 128, :],
                       wr=("wrf",) if c == 0 else ())
    pb.V("memset", ones_row[:], 1.0, waits=[tk_c])
    pb.V("tensor_copy", wrth[:], wrf[:])
    pb.V("tensor_tensor", wrtl[:], wrf[:], wrth[:], ALU.subtract)
    pb.V("tensor_copy", brh[:], brf[:])
    tk_wr = pb.V("tensor_tensor", brl[:], brf[:], brh[:], ALU.subtract)
    pb.barrier()

    trq = {}

    def thr(bkey, waits=()):
        q = trq.setdefault(bkey, [])
        w = list(waits)
        if q and q[-1] is not None:
            w.append(q[-1])
        return w

    def thr_push(bkey, ev):
        trq.setdefault(bkey, []).append(ev)

    def ln_stats(src_tile, par, junk, waits=()):
        tkr = pb.V("reduce_sum", mcol[par][:], src_tile, AX.X, waits=waits,
                   wr=(f"mc{par}",))
        tks = pb.S("activation", junk, src_tile, AF.Square,
                   accum_out=sscol[par][:], waits=waits, wr=(f"ss{par}", "junk" + str(par)))
        pb.V("tensor_scalar", mcol[par][:], mcol[par][:], 1.0 / 1024.0, None, ALU.mult,
             waits=[tkr])
        pb.V("tensor_tensor", v1col[par][:], mcol[par][:], mcol[par][:], ALU.mult)
        pb.V("tensor_scalar", sscol[par][:], sscol[par][:], 1.0 / 1024.0, None, ALU.mult,
             waits=[tks], rd=(f"ss{par}",))
        pb.V("tensor_tensor", v1col[par][:], sscol[par][:], v1col[par][:], ALU.subtract)
        tkv = pb.V("tensor_scalar", v1col[par][:], v1col[par][:], 1e-5, None, ALU.add,
                   wr=(f"v1{par}",))
        tksq = pb.S("activation", scol[par][:], v1col[par][:], AF.Sqrt, waits=[tkv],
                    rd=(f"v1{par}",), wr=(f"sc{par}",))
        tkrc = pb.V("reciprocal", scol[par][:], scol[par][:], waits=[tksq])
        return tkrc

    def ln_xpose(src_tile, tt, xtmpb, dstT, tkrc, affine=None, shift_cols=False,
                 extra_w=(), mset=None):
        par = tt % 2
        tkn = pb.V("tensor_scalar", xtmpb, src_tile, mcol[par][:], scol[par][:],
                   ALU.subtract, ALU.mult, waits=[tkrc] + list(extra_w),
                   wr=(f"xtmp{par}",), rd=(f"mc{par}", f"sc{par}"))
        evs = []
        for c in range(8):
            b = c % 2
            slq = (c // 2) % 4
            pr = f"pT{b}{slq}"
            tr = pb.T("transpose", psTs[b][:, slq * 256:slq * 256 + 128],
                      xtmpb[:, c * 128:(c + 1) * 128], ident[:],
                      waits=thr(("T", b), [tkn]),
                      rd=(f"xtmp{par}",), wr=(pr,))
            if shift_cols and c >= 4:
                ncols = 127 if tt == NT - 1 else 128
                dst = dstT[:, c, tt * 128 + 1: tt * 128 + 1 + ncols]
                srcp = psTs[b][:, slq * 256:slq * 256 + ncols]
            else:
                dst = dstT[:, c, tt * 128:(tt + 1) * 128]
                srcp = psTs[b][:, slq * 256:slq * 256 + 128]
            w2_ = [tr] + ([mset] if (shift_cols and c >= 4 and mset is not None) else [])
            if affine is not None:
                g, bb = affine
                ev = pb.S("activation", dst, srcp, AF.Identity, scale=g[:, c:c + 1],
                          bias=bb[:, c:c + 1], waits=w2_, rd=(pr,))
            else:
                ev = pb.S("tensor_copy", dst, srcp, waits=w2_, rd=(pr,))
            thr_push(("T", b), ev)
            evs.append(ev)
        return evs

    ZA = Zone(nc, 128 * KB, 176 * KB)
    xtile = [ZA.al(f"xtile{i}", [128, 1024], F32) for i in range(2)]
    xtmp = [ZA.al(f"xtmp{i}", [128, 1024], BF16) for i in range(2)]

    ln1_done = []
    for tt in range(NT):
        par = tt % 2
        xt = xtile[par]
        tkl = pb.emit("QG", "dma_start", out=xt[:], in_=x_in[tt * 128:(tt + 1) * 128, :],
                      wr=(f"xt{par}",))
        tkrc = ln_stats(xt[:], par, xtmp[par][:], waits=[tkl])
        evs = ln_xpose(xt[:], tt, xtmp[par][:], xnT, tkrc)
        for e in evs:
            pb.resr.setdefault(f"xt{par}", []).append(e)
        ln1_done += evs

    if "xnT1" in taps:
        tp = tap("xnT1", [1024, T])
        pb.emit("QG", "dma_start", out=tp.rearrange("(c a) t -> a c t", a=128),
                in_=xnT[:], waits=ln1_done)
    if upto < 1:
        finish(pb); return nc, pb, tap_out
    pb.barrier()

    ZB = Zone(nc, 128 * KB, 176 * KB)
    tabb = [ZB.al(f"tabb{i}", [128, 1024], BF16) for i in range(2)]
    wsl0 = [ZB.al(f"wsl0{i}", [128, 8, 128], BF16) for i in range(2)]
    wv_sb = ZB.al("wv_sb", [128, 8, 1024], BF16)
    qraw = ZB.al("qraw", [128, 512], BF16)
    t1b = ZB.al("t1b", [128, 512], BF16)
    rotb = ZB.al("rotb", [128, 512], BF16)

    QT = sb_at(nc, "QT", [128, DC, T], BF16, 32 * KB)
    KT = sb_at(nc, "KT", [128, DC, T], BF16, 64 * KB)
    Vsb = sb_at(nc, "Vsb", [128, NT, 1024], BF16, 96 * KB)

    for c in range(8):
        pb.emit("QG", "dma_start", out=wv_sb[:, c, :], in_=wv_in[c * 128:(c + 1) * 128, :],
                wr=("wv",) if c == 0 else ())
    tk_wv = ("QG", pb.cnt["QG"])

    qkv_done = []
    for part in range(2):
        wIn = wq_in if part == 0 else wk_in
        dstT = QT if part == 0 else KT
        tabn = ("l0cq", "l0sq") if part == 0 else ("l0ck", "l0sk")
        for c in range(8):
            wb = (part * 8 + c) % 2
            wsl = wsl0[wb]
            tkw = None
            for dc in range(8):
                tkw = pb.emit("QG", "dma_start", out=wsl[:, dc, :],
                              in_=wIn[dc * 128:(dc + 1) * 128, c * 128:(c + 1) * 128],
                              wr=(f"wsl0{wb}",) if dc == 0 else ())
            for ts in range(4):
                sl = slice(ts * 512, (ts + 1) * 512)
                tb = tabb[ts % 2]
                tkt = [pb.emit("QS", "dma_start", out=tb[:, 0:512],
                               in_=cin[tabn[0]][c * 128:(c + 1) * 128, sl],
                               wr=(f"tab{ts % 2}",)),
                       pb.emit("QS", "dma_start", out=tb[:, 512:1024],
                               in_=cin[tabn[1]][c * 128:(c + 1) * 128, sl])]
                mm = None
                for dc in range(8):
                    mm = pb.T("matmul", psA[:, sl], wsl[:, dc, :],
                              xnT[:, dc, sl], start=(dc == 0), stop=(dc == 7),
                              waits=[tkw] if dc == 0 else (),
                              rd=(f"wsl0{wb}",), wr=(f"pA{ts}",) if dc == 0 else ())
                tke = pb.S("tensor_copy", qraw[:], psA[:, sl], waits=[mm],
                           rd=(f"pA{ts}",), wr=("qraw",))
                tkr = pb.T("matmul", psB[:, 0:512], pint[:], qraw[:],
                           start=True, stop=True, waits=[tke], rd=("qraw",), wr=("pB0",))
                tkrb = pb.S("tensor_copy", rotb[:], psB[:, 0:512], waits=[tkr],
                            rd=("pB0",), wr=("rotb",))
                tk1 = pb.V("tensor_tensor", t1b[:], qraw[:], tb[:, 0:512],
                           ALU.mult, waits=[tke, tkt[0]], rd=("qraw", f"tab{ts % 2}"),
                           wr=("t1b",))
                tk2 = pb.V("tensor_tensor", dstT[:, c, sl], rotb[:], tb[:, 512:1024],
                           ALU.mult, waits=[tkrb, tkt[1]], rd=("rotb", f"tab{ts % 2}"))
                tk3 = pb.V("tensor_tensor", dstT[:, c, sl], dstT[:, c, sl], t1b[:],
                           ALU.add, waits=[tk1, tk2], rd=("t1b",))
                qkv_done.append(tk3)
    for tt in range(NT):
        for ds in range(2):
            mm = None
            for dc in range(8):
                mm = pb.T("matmul", psB[:, ds * 512:(ds + 1) * 512],
                          xnT[:, dc, tt * 128:(tt + 1) * 128],
                          wv_sb[:, dc, ds * 512:(ds + 1) * 512],
                          start=(dc == 0), stop=(dc == 7),
                          waits=[tk_wv] if (tt == 0 and dc == 0) else (),
                          rd=("wv",), wr=(f"pB{ds}",) if dc == 0 else ())
            tke = pb.S("tensor_copy", Vsb[:, tt, ds * 512:(ds + 1) * 512],
                       psB[:, ds * 512:(ds + 1) * 512], waits=[mm], rd=(f"pB{ds}",))
            qkv_done.append(tke)

    if "QT" in taps:
        pb.emit("QG", "dma_start", out=tap("QT", [1024, T]).rearrange("(c a) t -> a c t", a=128),
                in_=QT[:], waits=qkv_done)
    if "KT" in taps:
        pb.emit("QG", "dma_start", out=tap("KT", [1024, T]).rearrange("(c a) t -> a c t", a=128),
                in_=KT[:], waits=qkv_done)
    if "V" in taps:
        pb.emit("QG", "dma_start", out=tap("V", [T, 1024]).rearrange("(n a) d -> a n d", a=128),
                in_=Vsb[:], waits=qkv_done)
    if upto < 2:
        finish(pb); return nc, pb, tap_out
    pb.barrier()

    ZC = Zone(nc, 128 * KB, 176 * KB)
    dect = ZC.al("dect", [128, 3968], BF16)
    Pbuf = [ZC.al(f"Pbuf{i}", [128, 2048], BF16) for i in range(2)]
    PDT = ZC.al("PDT", [128, 17 * 128], BF16)
    lntmp = ZC.al("lntmp", [128, 1024], F32)
    lnbf = ZC.al("lnbf", [128, 1024], BF16)
    xct = [ZC.al(f"xct{i}", [128, 1024], F32) for i in range(2)]
    xnlo = ZC.al("xnlo", [128, 1024], BF16)

    pb.emit("QS", "dma_start", out=dect[:, 0:1984], in_=cin["dect"][:, 0:1984])
    tk_dect = pb.emit("QS", "dma_start", out=dect[:, 1984:3968], in_=cin["dect"][:, 1984:3968])

    x1_done = []
    rt_done = []
    for qt in range(16):
        par = qt % 2
        tkx = pb.emit("QG", "dma_start", out=xct[par][:],
                      in_=x_in[qt * 128:(qt + 1) * 128, :], wr=(f"xc{par}",))
        mm = None
        for ks in range(4):
            sl = slice(ks * 512, (ks + 1) * 512)
            for dc in range(8):
                mm = pb.T("matmul", psA[:, sl], QT[:, dc, qt * 128:(qt + 1) * 128],
                          KT[:, dc, sl], start=(dc == 0), stop=(dc == 7),
                          wr=("pA0", "pA1", "pA2", "pA3") if (ks == 0 and dc == 0) else ())
        tknm = pb.V("reduce_max", nmcol[:], psA[:], AX.X, negate=True, waits=[mm],
                    rd=("pA0",), wr=("nm",))
        P = Pbuf[par]
        tkexp = pb.S("activation", P[:], psA[:], AF.Exp, bias=nmcol[:],
                     accum_out=dncol[:], waits=[tknm],
                     rd=("pA0", "pA1", "pA2", "pA3", "nm"), wr=(f"P{par}", "dn"))
        tkrd = pb.V("reciprocal", rdcol[:], dncol[:], waits=[tkexp], rd=("dn",), wr=("rd",))
        j0 = 1920 - 128 * qt
        tkpd = pb.V("scalar_tensor_tensor", P[:], P[:], rdcol[:], dect[:, j0:j0 + 2048],
                    ALU.mult, ALU.mult, waits=[tkrd, tk_dect], rd=("rd",), wr=(f"P{par}",))
        kts = [kt for kt in range(16) if abs(kt - qt) <= 8]
        evs = {}
        for i, kt in enumerate(kts):
            b = i % 2
            slq = (i // 2) % 4
            tr = pb.T("transpose", psTs[b][:, slq * 256:slq * 256 + 128],
                      P[:, kt * 128:(kt + 1) * 128], ident[:],
                      waits=thr(("T", b), [tkpd]),
                      rd=(f"P{par}",), wr=(f"pT{b}{slq}",))
            ev = pb.V("tensor_copy", PDT[:, i * 128:(i + 1) * 128],
                      psTs[b][:, slq * 256:slq * 256 + 128], waits=[tr],
                      rd=(f"pT{b}{slq}",), wr=(f"pdt{i}",))
            thr_push(("T", b), ev)
            evs[kt] = (i, ev)
        pvt = []
        for ds in range(2):
            mm2 = None
            for jj_, kt in enumerate(kts):
                i, ev = evs[kt]
                mm2 = pb.T("matmul", psB[:, ds * 512:(ds + 1) * 512],
                           PDT[:, i * 128:(i + 1) * 128],
                           Vsb[:, kt, ds * 512:(ds + 1) * 512],
                           start=(jj_ == 0), stop=(jj_ == len(kts) - 1),
                           waits=[ev],
                           rd=(f"pdt{i}",), wr=(f"pB{ds}",) if jj_ == 0 else ())
            pvt.append(mm2)
        tkx1 = pb.V("tensor_tensor", xct[par][:, 0:512], xct[par][:, 0:512],
                    psB[:, 0:512], ALU.add, waits=[tkx, pvt[0]],
                    rd=("pB0",), wr=(f"xc{par}",))
        tkx2 = pb.V("tensor_tensor", xct[par][:, 512:1024], xct[par][:, 512:1024],
                    psB[:, 512:1024], ALU.add, waits=[tkx, pvt[1]], rd=("pB1",))
        tkw1 = pb.emit("QG", "dma_start", out=xres[qt * 128:(qt + 1) * 128, :],
                       in_=xct[par][:], waits=[tkx1, tkx2], rd=(f"xc{par}",))
        x1_done.append(tkw1)
        tkrc = ln_stats(xct[par][:], par, lnbf[:], waits=[tkx1, tkx2])
        tkn = pb.V("tensor_scalar", lntmp[:], xct[par][:], mcol[par][:], scol[par][:],
                   ALU.subtract, ALU.mult, waits=[tkrc], rd=(f"xc{par}", f"mc{par}", f"sc{par}"),
                   wr=("lntmp",))
        hilo = []
        for c in range(8):
            h_ = c % 2
            trf = pb.T("transpose", psB[:, h_ * 512:h_ * 512 + 128],
                       lntmp[:, c * 128:(c + 1) * 128], identf[:],
                       waits=thr(("B", h_), [tkn]), rd=("lntmp",), wr=(f"pB{h_}",))
            hi = pb.S("tensor_copy", xnT[:, c, qt * 128:(qt + 1) * 128],
                      psB[:, h_ * 512:h_ * 512 + 128], waits=[trf], rd=(f"pB{h_}",),
                      wr=(f"xh{c}",))
            lo = pb.V("tensor_tensor", xnlo[:, c * 128:(c + 1) * 128],
                      psB[:, h_ * 512:h_ * 512 + 128],
                      xnT[:, c, qt * 128:(qt + 1) * 128], ALU.subtract,
                      waits=[hi], rd=(f"pB{h_}", f"xh{c}"), wr=(f"xl{c}",))
            thr_push(("B", h_), lo)
            hilo += [hi, lo]
        mmr = None
        for c in range(8):
            mmr = pb.T("matmul", psB[:, 0:8], xnT[:, c, qt * 128:(qt + 1) * 128],
                       wrth[:, c * 8:(c + 1) * 8], start=(c == 0), stop=False,
                       waits=[tk_wr, hilo[2 * c]],
                       rd=(f"xh{c}",), wr=("pB0",) if c == 0 else ())
        for c in range(8):
            mmr = pb.T("matmul", psB[:, 0:8], xnlo[:, c * 128:(c + 1) * 128],
                       wrth[:, c * 8:(c + 1) * 8], start=False, stop=False,
                       waits=[hilo[2 * c + 1]], rd=(f"xl{c}",))
        for c in range(8):
            mmr = pb.T("matmul", psB[:, 0:8], xnT[:, c, qt * 128:(qt + 1) * 128],
                       wrtl[:, c * 8:(c + 1) * 8], start=False, stop=False, rd=(f"xh{c}",))
        mmr = pb.T("matmul", psB[:, 0:8], ones_row[0:1, 0:128], brh[0:1, :],
                   start=False, stop=False)
        mmr = pb.T("matmul", psB[:, 0:8], ones_row[0:1, 0:128], brl[0:1, :],
                   start=False, stop=True)
        tkm = pb.V("reduce_max", lgmax[:], psB[:, 0:8], AX.X, waits=[mmr], rd=("pB0",))
        tko = pb.V("tensor_single_scalar", ohtmp[:], psB[:, 0:8], lgmax[:], ALU.is_ge,
                   waits=[tkm], rd=("pB0",))
        tkc = pb.V("tensor_tensor_scan", ohc[:], ohtmp[:], ohtmp[:], 0.0,
                   ALU.add, ALU.bypass, waits=[tko])
        tkc2 = pb.V("tensor_single_scalar", ohc[:], ohc[:], 1.5, ALU.is_le, waits=[tkc])
        tkoh = pb.V("tensor_tensor", oh[:, qt * 8:(qt + 1) * 8], ohtmp[:], ohc[:],
                    ALU.mult, waits=[tkc2], rd=("pB0",))
        rt_done.append(tkoh)

    if "x1" in taps:
        pb.emit("QG", "dma_start", out=tap("x1", [T, 1024]), in_=xres[:], waits=x1_done)
    if "oh" in taps:
        pb.emit("QG", "dma_start", out=tap("oh", [128, 128]), in_=oh[:], waits=rt_done)
    if upto < 3:
        finish(pb); return nc, pb, tap_out
    pb.barrier()

    w1h = sb_at(nc, "w1h", [128, 8, 2048], BF16, 32 * KB)
    w2h = sb_at(nc, "w2h", [128, 16, 1024], BF16, 64 * KB)
    hTs = [sb_at(nc, f"hT{i}", [128, 16, 512], BF16, (96 + 16 * i) * KB)
           for i in range(2)]
    ZE = Zone(nc, 128 * KB, 176 * KB)
    mbb = ZE.al("mbb", [128, 2048], BF16)
    mrow = ZE.al("mrow", [1, 2048], BF16)
    b2row = ZE.al("b2row", [1, 1024], BF16)
    ybuf = [ZE.al(f"ybuf{i}", [128, 1024], F32) for i in range(2)]

    moe_done = []
    for e in range(8):
        tkb2 = pb.emit("QG", "dma_start", out=b2row[:], in_=b2r_in[e:e + 1, :], wr=("b2row",))
        mrow_toks = []
        for tt in range(16):
            h_ = tt % 2
            trf = pb.T("transpose", psB[0:1, h_ * 512:h_ * 512 + 128],
                       oh[:, tt * 8 + e:tt * 8 + e + 1],
                       identf[:], waits=thr(("B", h_), [rt_done[tt]]), wr=(f"pB{h_}",))
            ev = pb.V("tensor_copy", mrow[0:1, tt * 128:(tt + 1) * 128],
                      psB[0:1, h_ * 512:h_ * 512 + 128],
                      waits=[trf], rd=(f"pB{h_}",), wr=(f"mr{tt}",) if tt == 0 else ())
            thr_push(("B", h_), ev)
            mrow_toks.append(ev)
        pb.resw["mrow"] = mrow_toks[-1]
        mb_toks = []
        for ts in range(4):
            mmb = pb.T("matmul", psB[:, 512:1024], ones_row[0:1, 0:128],
                       mrow[0:1, ts * 512:(ts + 1) * 512], start=True, stop=True,
                       waits=[mrow_toks[-1]], wr=("pB1",))
            mb_toks.append(pb.V("tensor_copy", mbb[:, ts * 512:(ts + 1) * 512],
                                psB[:, 512:1024], waits=[mmb], rd=("pB1",),
                                wr=(f"mb{ts}",)))
        for fh in range(2):
            tk1 = None
            for dc in range(8):
                tk1 = pb.emit("QG", "dma_start", out=w1h[:, dc, :],
                              in_=w1t_in[e, dc * 128:(dc + 1) * 128,
                                         fh * 2048:(fh + 1) * 2048],
                              wr=("w1h",) if dc == 0 else ())
            tk2 = None
            for fc in range(16):
                tk2 = pb.emit("QG", "dma_start", out=w2h[:, fc, :],
                              in_=w2t_in[e, fh * 2048 + fc * 128:fh * 2048 + (fc + 1) * 128, :],
                              wr=("w2h",) if fc == 0 else ())
            for ts in range(4):
                sl = slice(ts * 512, (ts + 1) * 512)
                hT = hTs[ts % 2]
                htk = []
                for fc in range(16):
                    q_ = fc % 4
                    pgrp = psA[:, q_ * 512:(q_ + 1) * 512]
                    mm = None
                    for dc in range(8):
                        mm = pb.T("matmul", pgrp, w1h[:, dc, fc * 128:(fc + 1) * 128],
                                  xnT[:, dc, sl], start=(dc == 0), stop=(dc == 7),
                                  waits=[tk1] if dc == 0 else (),
                                  rd=("w1h",), wr=(f"pA{q_}",) if dc == 0 else ())
                    gfc = fh * 16 + fc
                    tkr = pb.S("activation", hT[:, fc, :], pgrp, AF.Relu,
                               bias=b1c_sb[:, e * 32 + gfc:e * 32 + gfc + 1], waits=[mm],
                               rd=(f"pA{q_}",), wr=(f"hT{ts % 2}_{fc}",))
                    tkm = pb.V("tensor_tensor", hT[:, fc, :], hT[:, fc, :], mbb[:, sl],
                               ALU.mult, waits=[tkr, mb_toks[ts]],
                               rd=(f"mb{ts}",), wr=(f"hT{ts % 2}_{fc}",))
                    htk.append(tkm)
                for tl in range(4):
                    tt = ts * 4 + tl
                    mm2 = None
                    for fc in range(16):
                        for ds in range(2):
                            mm2 = pb.T("matmul", psB[:, ds * 512:(ds + 1) * 512],
                                       hT[:, fc, tl * 128:(tl + 1) * 128],
                                       w2h[:, fc, ds * 512:(ds + 1) * 512],
                                       start=(fc == 0), stop=(fh == 1 and fc == 15),
                                       waits=[htk[fc], tk2],
                                       rd=("w2h", f"hT{ts % 2}_{fc}"),
                                       wr=("pB0", "pB1") if (fc == 0 and ds == 0) else ())
                    yb = ybuf[tl % 2]
                    if fh == 0:
                        r1a = pb.T("matmul", psB[:, 0:512],
                                   mrow[0:1, tt * 128:(tt + 1) * 128],
                                   b2row[0:1, 0:512], start=False, stop=True,
                                   waits=[tkb2], rd=("mrow", "b2row"))
                        r1b = pb.T("matmul", psB[:, 512:1024],
                                   mrow[0:1, tt * 128:(tt + 1) * 128],
                                   b2row[0:1, 512:1024], start=False, stop=True,
                                   rd=("mrow", "b2row"))
                        tke = pb.V("tensor_copy", yb[:], psB[:], waits=[r1a, r1b],
                                   rd=("pB0", "pB1"), wr=(f"yb{tl % 2}",))
                    else:
                        tke = pb.V("tensor_copy", yb[:], psB[:], waits=[mm2],
                                   rd=("pB0", "pB1"), wr=(f"yb{tl % 2}",))
                    tkacc = pb.emit("QG", "dma_start",
                                    out=xres[tt * 128:(tt + 1) * 128, :],
                                    in_=yb[:], accum_op=ALU.add, waits=[tke],
                                    rd=(f"yb{tl % 2}",))
                    moe_done.append(tkacc)

    if "x2" in taps:
        pb.emit("QG", "dma_start", out=tap("x2", [T, 1024]), in_=xres[:], waits=moe_done)
    if upto < 4:
        finish(pb); return nc, pb, tap_out
    pb.barrier()

    for tt in range(NT):
        par = tt % 2
        xt = xtile[par]
        tkl = pb.emit("QG", "dma_start", out=xt[:], in_=xres[tt * 128:(tt + 1) * 128, :],
                      wr=(f"xt{par}",))
        tkrc = ln_stats(xt[:], par, xtmp[par][:], waits=[tkl])
        evs = ln_xpose(xt[:], tt, xtmp[par][:], xnT, tkrc)
        for e_ in evs:
            pb.resr.setdefault(f"xt{par}", []).append(e_)
    ln3_done = evs
    if "xnT3" in taps:
        pb.emit("QG", "dma_start", out=tap("xnT3", [1024, T]).rearrange("(c a) t -> a c t", a=128),
                in_=xnT[:], waits=ln3_done)
    pb.barrier()

    qTa = sb_at(nc, "qTa", [128, DC, T], BF16, 32 * KB)
    kTa = sb_at(nc, "kTa", [128, DC, T], BF16, 64 * KB)
    vA = sb_at(nc, "vA", [128, NT, 1024], BF16, 96 * KB)
    wotsb = sb_at(nc, "wotsb", [128, 8, 1024], BF16, 128 * KB)
    atab = sb_at(nc, "atab", [128, 4, 2048], BF16, 144 * KB)
    ZG = Zone(nc, 160 * KB, 176 * KB)
    wqsl2 = [ZG.al(f"wqsl2{i}", [128, 8, 128], BF16) for i in range(2)]
    wvsl2 = ZG.al("wvsl2", [128, 8, 512], BF16)
    qraw2 = ZG.al("qraw2", [128, 512], BF16)
    t1a = ZG.al("t1a", [128, 512], BF16)
    rotba = ZG.al("rotba", [128, 512], BF16)

    for i, tn in enumerate(("acq", "asq", "ack", "ask")):
        pb.emit("QS", "dma_start", out=atab[:, i, :], in_=cin[tn][:],
                wr=("atab",) if i == 3 else ())
    for c in range(8):
        pb.emit("QG", "dma_start", out=wotsb[:, c, :], in_=wot_in[c * 128:(c + 1) * 128, :],
                wr=("wot",) if c == 0 else ())
    tk_wot = ("QG", pb.cnt["QG"])

    aproj_done = []
    for part in range(2):
        dstT = qTa if part == 0 else kTa
        coff = part * 1024
        for c in range(8):
            wb = c % 2
            wsl = wqsl2[wb]
            tkw = None
            for dc in range(8):
                tkw = pb.emit("QG", "dma_start", out=wsl[:, dc, :],
                              in_=wqkvt_in[dc * 128:(dc + 1) * 128,
                                           coff + c * 128:coff + (c + 1) * 128],
                              wr=(f"wq2{wb}",) if dc == 0 else ())
            for ts in range(4):
                sl = slice(ts * 512, (ts + 1) * 512)
                q_ = ts
                mm = None
                for dc in range(8):
                    mm = pb.T("matmul", psA[:, sl], wsl[:, dc, :],
                              xnT[:, dc, sl], start=(dc == 0), stop=(dc == 7),
                              waits=[tkw] if dc == 0 else (),
                              rd=(f"wq2{wb}",), wr=(f"pA{q_}",) if dc == 0 else ())
                tke = pb.S("tensor_copy", qraw2[:], psA[:, sl], waits=[mm],
                           rd=(f"pA{q_}",), wr=("qraw2",))
                tkr = pb.T("matmul", psB[:, 0:512], pint[:], qraw2[:],
                           start=True, stop=True, waits=[tke], rd=("qraw2",), wr=("pB0",))
                tkrb = pb.S("tensor_copy", rotba[:], psB[:, 0:512], waits=[tkr],
                            rd=("pB0",), wr=("rotba",))
                tk1 = pb.V("tensor_tensor", t1a[:], qraw2[:], atab[:, 2 * part, sl],
                           ALU.mult, waits=[tke], rd=("qraw2", "atab"), wr=("t1a",))
                tk2 = pb.V("tensor_tensor", dstT[:, c, sl], rotba[:], atab[:, 2 * part + 1, sl],
                           ALU.mult, waits=[tkrb], rd=("rotba", "atab"))
                tk3 = pb.V("tensor_tensor", dstT[:, c, sl], dstT[:, c, sl], t1a[:],
                           ALU.add, waits=[tk1, tk2], rd=("t1a",))
                aproj_done.append(tk3)
    for ds in range(2):
        wsl = wvsl2
        tkw = None
        for dc in range(8):
            tkw = pb.emit("QG", "dma_start", out=wsl[:, dc, :],
                          in_=wqkvt_in[dc * 128:(dc + 1) * 128,
                                       2048 + ds * 512:2048 + (ds + 1) * 512],
                          wr=("wv2",) if dc == 0 else ())
        for tt in range(NT):
            mm = None
            for dc in range(8):
                mm = pb.T("matmul", psB[:, 0:512], xnT[:, dc, tt * 128:(tt + 1) * 128],
                          wsl[:, dc, :], start=(dc == 0), stop=(dc == 7),
                          waits=[tkw] if dc == 0 else (), rd=("wv2",),
                          wr=("pB0",) if dc == 0 else ())
            tke = pb.S("tensor_copy", vA[:, tt, ds * 512:(ds + 1) * 512], psB[:, 0:512],
                       waits=[mm], rd=("pB0",))
            aproj_done.append(tke)

    if "qTa" in taps:
        pb.emit("QG", "dma_start", out=tap("qTa", [1024, T]).rearrange("(c a) t -> a c t", a=128),
                in_=qTa[:], waits=aproj_done)
    if "kTa" in taps:
        pb.emit("QG", "dma_start", out=tap("kTa", [1024, T]).rearrange("(c a) t -> a c t", a=128),
                in_=kTa[:], waits=aproj_done)
    if "vA" in taps:
        pb.emit("QG", "dma_start", out=tap("vA", [T, 1024]).rearrange("(n a) d -> a n d", a=128),
                in_=vA[:], waits=aproj_done)
    pb.barrier()

    ZG2 = Zone(nc, 144 * KB, 176 * KB)
    Pa = [ZG2.al(f"Pa{i}", [128, 2048], BF16) for i in range(2)]
    PTa = ZG2.al("PTa", [128, 16, 512], BF16)
    oTa = ZG2.al("oTa", [128, 8, 512], BF16)
    rdh = rdcol
    xg = [sb_at(nc, f"xg{i}", [128, 1024], F32, i * 4 * KB)
          for i in range(2)]

    l1a_done = []
    for qs in range(4):
        oT_evacs = []
        for hp in range(8):
            pvx = []
            for h2 in range(2):
                h = 2 * hp + h2
                base = (h % 2) * 64
                hc = h // 2
                pt_last = {}
                for ql in range(4):
                    qt = qs * 4 + ql
                    nk = 128 * (qt + 1)
                    mm = None
                    for ks in range((nk + 511) // 512):
                        n0 = ks * 512
                        n1 = min(nk, n0 + 512)
                        mm = pb.T("matmul", psA[:, n0:n1],
                                  qTa[base:base + 64, hc, qt * 128:(qt + 1) * 128],
                                  kTa[base:base + 64, hc, n0:n1],
                                  start=True, stop=True,
                                  wr=("pA0", "pA1", "pA2", "pA3") if ks == 0 else ())
                    tkd = pb.V("tensor_tensor", psA[:, nk - 128:nk], psA[:, nk - 128:nk],
                               diagm[:], ALU.add, waits=[mm], rd=("pA0",))
                    tknm = pb.V("reduce_max", nmcol[:], psA[:, 0:nk], AX.X, negate=True,
                                waits=[tkd], rd=("pA0",), wr=("nm",))
                    P = Pa[(qt + h) % 2]
                    pn = f"Pa{(qt + h) % 2}"
                    tkexp = pb.S("activation", P[:, 0:nk], psA[:, 0:nk], AF.Exp,
                                 bias=nmcol[:], accum_out=dncol[:], waits=[tknm],
                                 rd=("pA0", "pA1", "pA2", "pA3", "nm"), wr=(pn, "dn"))
                    tkrd = pb.V("reciprocal", rdh[:], dncol[:], waits=[tkexp],
                                rd=("dn",), wr=("rd",))
                    tkpn = pb.V("tensor_scalar", P[:, 0:nk], P[:, 0:nk], rdh[:], None,
                                ALU.mult, waits=[tkrd], rd=("rd",), wr=(pn,))
                    for kt in range(qt + 1):
                        b = kt % 2
                        slq = (kt // 2) % 4
                        tr = pb.T("transpose", psTs[b][:, slq * 256:slq * 256 + 128],
                                  P[:, kt * 128:(kt + 1) * 128], ident[:],
                                  waits=thr(("T", b), [tkpn]),
                                  rd=(pn,), wr=(f"pT{b}{slq}",))
                        ev = pb.V("tensor_copy", PTa[:, kt, ql * 128:(ql + 1) * 128],
                                  psTs[b][:, slq * 256:slq * 256 + 128], waits=[tr],
                                  rd=(f"pT{b}{slq}",), wr=(f"PTa{kt}_{ql}",))
                        thr_push(("T", b), ev)
                        pt_last[(kt, ql)] = ev
                for ql in range(4):
                    qt = qs * 4 + ql
                    mm2 = None
                    for kt in range(qt + 1):
                        mm2 = pb.T("matmul",
                                   psB[base:base + 64, ql * 128:(ql + 1) * 128],
                                   vA[:, kt, h * 64:(h + 1) * 64],
                                   PTa[:, kt, ql * 128:(ql + 1) * 128],
                                   start=(kt == 0), stop=(kt == qt),
                                   waits=[pt_last[(kt, ql)]],
                                   rd=(f"PTa{kt}_{ql}",),
                                   wr=("pB0",) if kt == 0 else ())
                    pvx.append(mm2)
            ev = pb.V("tensor_copy", oTa[:, hp, :], psB[:, 0:512], waits=pvx[0:8:4] + pvx[3::4],
                      rd=("pB0",), wr=(f"oTa{hp}",))
            ev2 = pb.V("tensor_copy", oTa[:, hp, :], oTa[:, hp, :], rd=(f"oTa{hp}",))
            oT_evacs.append(ev)
        for ql in range(4):
            tt = qs * 4 + ql
            par = tt % 2
            tkx = pb.emit("QG", "dma_start", out=xg[par][:],
                          in_=xres[tt * 128:(tt + 1) * 128, :], wr=(f"xg{par}",))
            mm = None
            for ds in range(2):
                for hp in range(8):
                    mm = pb.T("matmul", psB[:, ds * 512:(ds + 1) * 512],
                              oTa[:, hp, ql * 128:(ql + 1) * 128],
                              wotsb[:, hp, ds * 512:(ds + 1) * 512],
                              start=(hp == 0), stop=(hp == 7),
                              waits=[tk_wot, oT_evacs[hp]],
                              rd=("wot", f"oTa{hp}"),
                              wr=(f"pB{ds}",) if hp == 0 else ())
            tkx1 = pb.V("tensor_tensor", xg[par][:, 0:512], xg[par][:, 0:512],
                        psB[:, 0:512], ALU.add, waits=[tkx, mm], rd=("pB0",), wr=(f"xg{par}",))
            tkx2 = pb.V("tensor_tensor", xg[par][:, 512:1024], xg[par][:, 512:1024],
                        psB[:, 512:1024], ALU.add, waits=[tkx, mm], rd=("pB1",))
            tkw1 = pb.emit("QG", "dma_start", out=xres[tt * 128:(tt + 1) * 128, :],
                           in_=xg[par][:], waits=[tkx1, tkx2], rd=(f"xg{par}",))
            l1a_done.append(tkw1)

    if "x3" in taps:
        pb.emit("QG", "dma_start", out=tap("x3", [T, 1024]), in_=xres[:], waits=l1a_done)
    if upto < 5:
        finish(pb); return nc, pb, tap_out
    pb.barrier()

    zsb = sb_at(nc, "zsb", [128, NT, 1024], F32, 0)
    hnT = sb_at(nc, "hnT", [128, DC, T], BF16, 64 * KB)
    qTf = sb_at(nc, "qTf", [128, 4, T], BF16, 96 * KB)
    kTf = sb_at(nc, "kTf", [128, 4, T], BF16, 112 * KB)
    vF = sb_at(nc, "vF", [128, NT, 512], BF16, 128 * KB)
    ftab = sb_at(nc, "ftab", [128, 4, 2048], BF16, 144 * KB)
    fwsl_t = sb_at(nc, "fwsl", [128, 8, 512], BF16, 160 * KB)
    fwsl = [fwsl_t, fwsl_t]
    qktmp = [sb_at(nc, f"qktmp{i}", [128, 512], BF16,
                                     offset=(168 + i) * KB) for i in range(2)]
    t1f = sb_at(nc, "t1f", [128, 512], BF16, 170 * KB)
    rotbf = sb_at(nc, "rotbf", [128, 512], BF16, 171 * KB)
    w1asl = [sb_at(nc, f"w1asl{i}", [128, 8, 128], BF16,
                                     offset=(162 + 2 * i) * KB) for i in range(2)]
    w1gsl = [sb_at(nc, f"w1gsl{i}", [128, 8, 128], BF16,
                                     offset=(166 + 2 * i) * KB) for i in range(2)]
    h2b = [sb_at(nc, f"h2b{i}", [128, 512], F32,
                                   offset=(172 + 2 * i) * KB) for i in range(2)]

    z_tok = {}
    for tt in range(NT):
        par = tt % 2
        xt = xtile[par]
        tkl = pb.emit("QG", "dma_start", out=xt[:], in_=xres[tt * 128:(tt + 1) * 128, :],
                      wr=(f"xt{par}",))
        tkrc = ln_stats(xt[:], par, xtmp[par][:], waits=[tkl])
        tkn = pb.V("tensor_scalar", zsb[:, tt, :], xt[:], mcol[par][:], scol[par][:],
                   ALU.subtract, ALU.mult, waits=[tkrc],
                   rd=(f"xt{par}", f"mc{par}", f"sc{par}"), wr=(f"z{tt}",))
        pb.resr.setdefault(f"xt{par}", []).append(tkn)
        z_tok[tt] = tkn
    if "z0" in taps:
        pb.emit("QG", "dma_start", out=tap("z0", [T, 1024]).rearrange("(n a) d -> a n d", a=128),
                in_=zsb[:], waits=[z_tok[t] for t in range(NT)])

    for layer in range(2):
        pb.barrier()
        mset = pb.V("memset", hnT[:, 4, 0:1], 0.0)
        for c in range(5, 8):
            mset = pb.V("memset", hnT[:, c, 0:1], 0.0)
        sh_done = []
        for tt in range(NT):
            par = tt % 2
            tkc = pb.V("tensor_copy", xtmp[par][:], zsb[:, tt, :],
                       waits=[z_tok[tt]], wr=(f"xtmp{par}",))
            for c in range(8):
                b = c % 2
                slq = (c // 2) % 4
                tr = pb.T("transpose", psTs[b][:, slq * 256:slq * 256 + 128],
                          xtmp[par][:, c * 128:(c + 1) * 128], ident[:],
                          waits=thr(("T", b), [tkc]),
                          rd=(f"xtmp{par}",), wr=(f"pT{b}{slq}",))
                if c >= 4:
                    ncols = 127 if tt == NT - 1 else 128
                    ev = pb.S("tensor_copy", hnT[:, c, tt * 128 + 1:tt * 128 + 1 + ncols],
                              psTs[b][:, slq * 256:slq * 256 + ncols],
                              waits=[tr, mset], rd=(f"pT{b}{slq}",))
                else:
                    ev = pb.S("tensor_copy", hnT[:, c, tt * 128:(tt + 1) * 128],
                              psTs[b][:, slq * 256:slq * 256 + 128],
                              waits=[tr], rd=(f"pT{b}{slq}",))
                thr_push(("T", b), ev)
                sh_done.append(ev)
        tktab = None
        for i, tn in enumerate(("fcq", "fsq", "fck", "fsk")):
            tktab = pb.emit("QS", "dma_start", out=ftab[:, i, :], in_=cin[tn][:],
                            wr=("ftab",) if i == 0 else ())
        qk_done = []
        for part in range(3):
            wsl = fwsl[part % 2]
            tkw = None
            for dc in range(8):
                tkw = pb.emit("QG", "dma_start", out=wsl[:, dc, :],
                              in_=fwqkv_in[layer, dc * 128:(dc + 1) * 128,
                                           part * 512:(part + 1) * 512],
                              wr=("fw0",) if dc == 0 else ())
            for tt in range(NT):
                mm = None
                for dc in range(8):
                    mm = pb.T("matmul", psB[:, 0:512], hnT[:, dc, tt * 128:(tt + 1) * 128],
                              wsl[:, dc, :], start=(dc == 0), stop=(dc == 7),
                              waits=[tkw, sh_done[-1]] if dc == 0 else (),
                              rd=("fw0",), wr=("pB0",) if dc == 0 else ())
                if part == 2:
                    tke = pb.S("tensor_copy", vF[:, tt, :], psB[:, 0:512], waits=[mm],
                               rd=("pB0",))
                    qk_done.append(tke)
                    continue
                tks = []
                for h in range(8):
                    tks.append(pb.S("activation", xtmp[0][:, h * 64:(h + 1) * 64],
                                    psB[:, h * 64:(h + 1) * 64], AF.Square,
                                    accum_out=qss[:, h:h + 1], waits=[mm],
                                    rd=("pB0",), wr=("qss", "xtmp0") if h == 0 else ("qss",)))
                tsq = pb.S("activation", qrn[:], qss[:], AF.Sqrt, waits=tks,
                           rd=("qss",), wr=("qrn",))
                tmx = pb.V("tensor_single_scalar", qrn[:], qrn[:], 1e-12, ALU.max,
                           waits=[tsq], wr=("qrn",))
                trc = pb.V("reciprocal", qrn[:], qrn[:], waits=[tmx], wr=("qrn",))
                qkt = qktmp[tt % 2]
                tnn = []
                for h in range(8):
                    tnn.append(pb.V("tensor_scalar", qkt[:, h * 64:(h + 1) * 64],
                                    psB[:, h * 64:(h + 1) * 64], qrn[:, h:h + 1], None,
                                    ALU.mult, waits=[trc], rd=("pB0", "qrn"),
                                    wr=(f"qkt{tt % 2}",) if h == 0 else ()))
                dstT = qTf if part == 0 else kTf
                scol_ = qsks_sb[:, 2 * layer + part:2 * layer + part + 1]
                for c in range(4):
                    b = c % 2
                    slq = (c // 2) % 4
                    tr = pb.T("transpose", psTs[b][:, slq * 256:slq * 256 + 128],
                              qkt[:, c * 128:(c + 1) * 128], ident[:],
                              waits=thr(("T", b), tnn),
                              rd=(f"qkt{tt % 2}",), wr=(f"pT{b}{slq}",))
                    ev = pb.S("activation", dstT[:, c, tt * 128:(tt + 1) * 128],
                              psTs[b][:, slq * 256:slq * 256 + 128], AF.Identity,
                              scale=scol_, waits=[tr], rd=(f"pT{b}{slq}",))
                    thr_push(("T", b), ev)
                    qk_done.append(ev)
        rot_done = []
        for part in range(2):
            dstT = qTf if part == 0 else kTf
            for c in range(4):
                for ts in range(4):
                    sl = slice(ts * 512, (ts + 1) * 512)
                    tkr = pb.T("matmul", psB[:, 0:512], psplit[:], dstT[:, c, sl],
                               start=True, stop=True,
                               waits=[qk_done[-1], tktab], wr=("pB0",))
                    tkrb = pb.S("tensor_copy", rotbf[:], psB[:, 0:512], waits=[tkr],
                                rd=("pB0",), wr=("rotbf",))
                    tk1 = pb.V("tensor_tensor", t1f[:], dstT[:, c, sl],
                               ftab[:, 2 * part, sl], ALU.mult,
                               waits=[tkr], rd=("ftab",), wr=("t1f",))
                    tk2 = pb.V("tensor_tensor", dstT[:, c, sl], rotbf[:],
                               ftab[:, 2 * part + 1, sl], ALU.mult,
                               waits=[tkrb, tk1], rd=("rotbf", "ftab"))
                    tk3 = pb.V("tensor_tensor", dstT[:, c, sl], dstT[:, c, sl], t1f[:],
                               ALU.add, waits=[tk2], rd=("t1f",))
                    rot_done.append(tk3)
        pb.barrier()
        if layer == 0:
            ZH2 = Zone(nc, 144 * KB, 160 * KB)
            Pf = [ZH2.al(f"Pf{i}", [128, 1024], BF16) for i in range(2)]
            PTf = ZH2.al("PTf", [128, 8, 512], BF16)
            oTf = ZH2.al("oTf", [128, 4, 512], BF16)
            rdf = rdcol
            wosb = sb_at(nc, "wosb", [128, 4, 1024], BF16, 160 * KB)
        tkwo = None
        for c in range(4):
            tkwo = pb.emit("QG", "dma_start", out=wosb[:, c, :],
                           in_=fwo_in[layer, c * 128:(c + 1) * 128, :],
                           wr=("wosb",) if c == 0 else ())
        for w in range(4):
            k0t = max(0, (w - 1) * 4)
            nkt = 4 if w == 0 else 8
            pv_toks = []
            for h in range(8):
                base = (h % 2) * 64
                hc = h // 2
                pt_last = {}
                for ql in range(4):
                    qt = w * 4 + ql
                    nk = nkt * 128
                    pa0 = (h + ql) % 2
                    mm = None
                    for ks in range(nkt // 4):
                        mm = pb.T("matmul", psA[:, pa0 * 1024 + ks * 512:
                                               pa0 * 1024 + (ks + 1) * 512],
                                  qTf[base:base + 64, hc, qt * 128:(qt + 1) * 128],
                                  kTf[base:base + 64, hc,
                                      k0t * 128 + ks * 512:k0t * 128 + (ks + 1) * 512],
                                  start=True, stop=True,
                                  wr=(f"pA{pa0*2}", f"pA{pa0*2+1}") if ks == 0 else ())
                    j0 = 384 - 128 * ql + (512 if w == 0 else 0)
                    tkd = pb.V("tensor_tensor", psA[:, pa0 * 1024:pa0 * 1024 + nk],
                               psA[:, pa0 * 1024:pa0 * 1024 + nk],
                               fmask[:, j0:j0 + nk], ALU.add, waits=[mm],
                               rd=(f"pA{pa0*2}",))
                    tknm = pb.V("reduce_max", nmcol[:], psA[:, pa0 * 1024:pa0 * 1024 + nk],
                                AX.X, negate=True, waits=[tkd], rd=(f"pA{pa0*2}",),
                                wr=("nm",))
                    P = Pf[(h + ql) % 2]
                    pn = f"Pf{(h + ql) % 2}"
                    tkexp = pb.S("activation", P[:, 0:nk], psA[:, pa0 * 1024:pa0 * 1024 + nk],
                                 AF.Exp, bias=nmcol[:], accum_out=dncol[:], waits=[tknm],
                                 rd=(f"pA{pa0*2}", f"pA{pa0*2+1}", "nm"), wr=(pn, "dn"))
                    tkrd = pb.V("reciprocal", rdf[:], dncol[:], waits=[tkexp],
                                rd=("dn",), wr=("rd",))
                    tkpn = pb.V("tensor_scalar", P[:, 0:nk], P[:, 0:nk], rdf[:], None,
                                ALU.mult, waits=[tkrd], rd=("rd",), wr=(pn,))
                    for kt in range(nkt):
                        b = kt % 2
                        slq = (kt // 2) % 4
                        tr = pb.T("transpose", psTs[b][:, slq * 256:slq * 256 + 128],
                                  P[:, kt * 128:(kt + 1) * 128], ident[:],
                                  waits=thr(("T", b), [tkpn]),
                                  rd=(pn,), wr=(f"pT{b}{slq}",))
                        ev = pb.V("tensor_copy", PTf[:, kt, ql * 128:(ql + 1) * 128],
                                  psTs[b][:, slq * 256:slq * 256 + 128], waits=[tr],
                                  rd=(f"pT{b}{slq}",), wr=(f"PTf{kt}_{ql}",))
                        thr_push(("T", b), ev)
                        pt_last[(kt, ql)] = ev
                mm2 = None
                for kt in range(nkt):
                    mm2 = pb.T("matmul", psB[base:base + 64, 0:512],
                               vF[:, k0t + kt, h * 64:(h + 1) * 64],
                               PTf[:, kt, 0:512],
                               start=(kt == 0), stop=(kt == nkt - 1),
                               waits=[pt_last[(kt, 0)], pt_last[(kt, 3)]],
                               rd=tuple(f"PTf{kt}_{q_}" for q_ in range(4)),
                               wr=("pB0",) if kt == 0 else ())
                pv_toks.append(mm2)
                if h % 2 == 1:
                    ev = pb.V("tensor_copy", oTf[:, h // 2, :], psB[:, 0:512],
                              waits=pv_toks[-2:], rd=("pB0",), wr=(f"oTf{h // 2}",))
                    pv_toks.append(ev)
            for ql in range(4):
                tt = w * 4 + ql
                mm = None
                for ds in range(2):
                    for jc in range(4):
                        mm = pb.T("matmul", psB[:, ds * 512:(ds + 1) * 512],
                                  oTf[:, jc, ql * 128:(ql + 1) * 128],
                                  wosb[:, jc, ds * 512:(ds + 1) * 512],
                                  start=(jc == 0), stop=(jc == 3),
                                  waits=[tkwo, pv_toks[-1]],
                                  rd=("wosb", f"oTf{jc}"),
                                  wr=(f"pB{ds}",) if jc == 0 else ())
                tkz = pb.V("tensor_tensor", zsb[:, tt, :], zsb[:, tt, :], psB[:],
                           ALU.add, waits=[mm, z_tok[tt]], rd=("pB0", "pB1"),
                           wr=(f"z{tt}",))
                z_tok[tt] = tkz
        if f"zi{layer}" in taps:
            pb.emit("QG", "dma_start",
                    out=tap(f"zi{layer}", [T, 1024]).rearrange("(n a) d -> a n d", a=128),
                    in_=zsb[:], waits=[z_tok[t] for t in range(NT)])
        pb.barrier()
        iln_done = []
        for tt in range(NT):
            par = tt % 2
            hb = h2b[par]
            wz = [z_tok[tt]] + ([z_tok[tt - 1]] if tt > 0 else [])
            tk1_ = pb.emit("QG", "dma_start", out=hb[1:128, :],
                           in_=zsb[0:127, tt, 512:1024], waits=wz, wr=(f"hb{par}",))
            if tt > 0:
                tk2_ = pb.emit("QG", "dma_start", out=hb[0:1, :],
                               in_=zsb[127:128, tt - 1, 512:1024])
            else:
                tk2_ = pb.V("memset", hb[0:1, :], 0.0, rd=(f"hb{par}",))
            tra = pb.V("reduce_sum", s1c[par][:], zsb[:, tt, 0:512], AX.X, waits=wz,
                       wr=(f"s1{par}",))
            trb = pb.V("reduce_sum", s2c[par][:], hb[:], AX.X, waits=[tk1_, tk2_],
                       rd=(f"hb{par}",), wr=(f"s2{par}",))
            pb.V("tensor_tensor", mcol[par][:], s1c[par][:], s2c[par][:], ALU.add,
                 waits=[tra, trb], rd=(f"s1{par}", f"s2{par}"), wr=(f"mc{par}",))
            tsa = pb.S("activation", xtmp[0][:, 0:512], zsb[:, tt, 0:512], AF.Square,
                       accum_out=s1c[par][:], waits=[tra] + wz, wr=("xtmp0", f"s1{par}"))
            tsb = pb.S("activation", xtmp[0][:, 512:1024], hb[:], AF.Square,
                       accum_out=s2c[par][:], waits=[trb], rd=(f"hb{par}",),
                       wr=("xtmp0", f"s2{par}"))
            pb.V("tensor_scalar", mcol[par][:], mcol[par][:], 1.0 / 1024.0, None, ALU.mult)
            pb.V("tensor_tensor", v1col[par][:], mcol[par][:], mcol[par][:], ALU.mult,
                 wr=(f"v1{par}",))
            tss = pb.V("tensor_tensor", sscol[par][:], s1c[par][:], s2c[par][:], ALU.add,
                       waits=[tsa, tsb], rd=(f"s1{par}", f"s2{par}"), wr=(f"ss{par}",))
            pb.V("tensor_scalar", sscol[par][:], sscol[par][:], 1.0 / 1024.0, None,
                 ALU.mult, waits=[tss])
            pb.V("tensor_tensor", v1col[par][:], sscol[par][:], v1col[par][:],
                 ALU.subtract, rd=(f"ss{par}",))
            tkv = pb.V("tensor_scalar", v1col[par][:], v1col[par][:], 1e-5, None, ALU.add,
                       wr=(f"v1{par}",))
            tksq = pb.S("activation", scol[par][:], v1col[par][:], AF.Sqrt, waits=[tkv],
                        rd=(f"v1{par}",), wr=(f"sc{par}",))
            tkrc = pb.V("reciprocal", scol[par][:], scol[par][:], waits=[tksq])
            xm = xtmp[1]
            tn1 = pb.V("tensor_scalar", xm[:, 0:512], zsb[:, tt, 0:512], mcol[par][:],
                       scol[par][:], ALU.subtract, ALU.mult, waits=[tkrc],
                       rd=(f"mc{par}", f"sc{par}"), wr=("xtmp1",))
            tn2 = pb.V("tensor_scalar", xm[:, 512:1024], hb[:], mcol[par][:], scol[par][:],
                       ALU.subtract, ALU.mult, waits=[tkrc],
                       rd=(f"mc{par}", f"sc{par}", f"hb{par}"))
            for c in range(8):
                b = c % 2
                slq = (c // 2) % 4
                tr = pb.T("transpose", psTs[b][:, slq * 256:slq * 256 + 128],
                          xm[:, c * 128:(c + 1) * 128], ident[:],
                          waits=thr(("T", b), [tn1, tn2]),
                          rd=("xtmp1",), wr=(f"pT{b}{slq}",))
                ev = pb.S("activation", hnT[:, c, tt * 128:(tt + 1) * 128],
                          psTs[b][:, slq * 256:slq * 256 + 128], AF.Identity,
                          scale=lng_sb[:, 8 * layer + c:8 * layer + c + 1],
                          bias=lnb_sb[:, 8 * layer + c:8 * layer + c + 1],
                          waits=[tr], rd=(f"pT{b}{slq}",))
                thr_push(("T", b), ev)
                iln_done.append(ev)
        if f"hn{layer}" in taps:
            pb.emit("QG", "dma_start",
                    out=tap(f"hn{layer}", [1024, T]).rearrange("(c a) t -> a c t", a=128),
                    in_=hnT[:], waits=iln_done)
        pb.barrier()
        if layer == 0:
            h1a = sb_at(nc, "h1a", [128, 22, 512], BF16, 96 * KB)
            h1g = sb_at(nc, "h1g", [128, 22, 512], BF16, 118 * KB)
            w2sl = [sb_at(nc, f"w2sl{i}", [128, 22, 256], BF16,
                                            offset=(140 + 11 * i) * KB) for i in range(2)]
        for ts in range(4):
            sl = slice(ts * 512, (ts + 1) * 512)
            glu_toks = []
            for fc in range(22):
                q_ = fc % 4
                pgrp = psA[:, q_ * 512:(q_ + 1) * 512]
                tka = None
                tkg = None
                for dc in range(8):
                    tka = pb.emit("QG", "dma_start", out=w1asl[fc % 2][:, dc, :],
                                  in_=fw1a_in[layer, dc * 128:(dc + 1) * 128,
                                              fc * 128:(fc + 1) * 128],
                                  wr=(f"w1a{fc % 2}",) if dc == 0 else ())
                    tkg = pb.emit("QG", "dma_start", out=w1gsl[fc % 2][:, dc, :],
                                  in_=fw1g_in[layer, dc * 128:(dc + 1) * 128,
                                              fc * 128:(fc + 1) * 128],
                                  wr=(f"w1g{fc % 2}",) if dc == 0 else ())
                mm = None
                for dc in range(8):
                    mm = pb.T("matmul", pgrp, w1asl[fc % 2][:, dc, :], hnT[:, dc, sl],
                              start=(dc == 0), stop=(dc == 7),
                              waits=[tka, iln_done[-1]] if dc == 0 else (),
                              rd=(f"w1a{fc % 2}",), wr=(f"pA{q_}",) if dc == 0 else ())
                eva = pb.S("activation", h1a[:, fc, :], pgrp, AF.Identity,
                           bias=fb1a_sb[:, 22 * layer + fc:22 * layer + fc + 1], waits=[mm],
                           rd=(f"pA{q_}",), wr=(f"ha{fc}",))
                mm = None
                for dc in range(8):
                    mm = pb.T("matmul", pgrp, w1gsl[fc % 2][:, dc, :], hnT[:, dc, sl],
                              start=(dc == 0), stop=(dc == 7),
                              waits=[tkg, eva] if dc == 0 else (),
                              rd=(f"w1g{fc % 2}",), wr=(f"pA{q_}",) if dc == 0 else ())
                evg = pb.S("activation", h1g[:, fc, :], pgrp, AF.Gelu,
                           bias=fb1g_sb[:, 22 * layer + fc:22 * layer + fc + 1], waits=[mm],
                           rd=(f"pA{q_}",), wr=(f"hg{fc}",))
                glu_toks.append(pb.V("tensor_tensor", h1a[:, fc, :], h1a[:, fc, :],
                                     h1g[:, fc, :], ALU.mult, waits=[eva, evg],
                                     rd=(f"hg{fc}",), wr=(f"ha{fc}",)))
            for dq in range(4):
                tkw2 = None
                for fc in range(22):
                    tkw2 = pb.emit("QG", "dma_start", out=w2sl[dq % 2][:, fc, :],
                                   in_=fw2_in[layer, fc * 128:(fc + 1) * 128,
                                              dq * 256:(dq + 1) * 256],
                                   wr=(f"w2s{dq % 2}",) if fc == 0 else ())
                for tl in range(4):
                    tt = ts * 4 + tl
                    mm2 = None
                    for fc in range(22):
                        mm2 = pb.T("matmul", psB[:, tl * 256:(tl + 1) * 256],
                                   h1a[:, fc, tl * 128:(tl + 1) * 128],
                                   w2sl[dq % 2][:, fc, :],
                                   start=(fc == 0), stop=False,
                                   waits=[glu_toks[fc], tkw2],
                                   rd=(f"ha{fc}", f"w2s{dq % 2}"),
                                   wr=(f"pB{tl // 2}",) if fc == 0 else ())
                    r1 = pb.T("matmul", psB[:, tl * 256:(tl + 1) * 256],
                              ones_row[0:1, 0:128],
                              fb2_sb[0:1, layer * 1024 + dq * 256:layer * 1024 + (dq + 1) * 256],
                              start=False, stop=True)
                    tkz = pb.V("tensor_tensor", zsb[:, tt, dq * 256:(dq + 1) * 256],
                               zsb[:, tt, dq * 256:(dq + 1) * 256],
                               psB[:, tl * 256:(tl + 1) * 256], ALU.add,
                               waits=[r1, z_tok[tt]], rd=(f"pB{tl // 2}",))
                    if dq == 3:
                        z_tok[tt] = tkz
        if f"zo{layer}" in taps:
            pb.emit("QG", "dma_start",
                    out=tap(f"zo{layer}", [T, 1024]).rearrange("(n a) d -> a n d", a=128),
                    in_=zsb[:], waits=[z_tok[t] for t in range(NT)])
        pb.barrier()

    fin = []
    for tt in range(NT):
        par = tt % 2
        xt = xtile[par]
        tkl = pb.emit("QG", "dma_start", out=xt[:], in_=xres[tt * 128:(tt + 1) * 128, :],
                      wr=(f"xt{par}",))
        tko = pb.V("tensor_tensor", xt[:], xt[:], zsb[:, tt, :], ALU.add,
                   waits=[tkl, z_tok[tt]], wr=(f"xt{par}",))
        tkw = pb.emit("QG", "dma_start", out=y_out[tt * 128:(tt + 1) * 128, :], in_=xt[:],
                      waits=[tko], rd=(f"xt{par}",))
        fin += [tko, tkw]
    finish(pb)
    return nc, pb, tap_out


def finish(pb):
    snap = [(k, pb.cnt[k]) for k in ("QG", "QC", "QS") if pb.cnt[k] > 0]
    pb.G("engine_nop", waits=snap)


def make_inmaps(inputs, taps=()):
    g = host_layout(inputs)
    cbf, cf32 = host_consts()
    base = {}
    for k, v in g.items():
        base[k] = np.ascontiguousarray(v, dtype=np.float32)
    for k, v in cbf.items():
        base[k] = v
    for k, v in cf32.items():
        base[k] = v
    X = inputs["L0_attn_X"]
    return [dict(base, x0=np.ascontiguousarray(X[b])) for b in range(8)]


def kernel(**inputs):
    nc, pb, _ = build_program(upto=99)
    replay(nc, pb)
    in_maps = make_inmaps(inputs)
    res = run_bass_kernel_spmd(nc, in_maps, core_ids=list(range(8)))
    out = np.stack([res.results[b]["y"] for b in range(8)], axis=0)
    return out.astype(np.float32)
